# version 01
# baseline (speedup 1.0000x reference)
import numpy as np
import ml_dtypes
import concourse.bass as bass
import concourse.mybir as mybir
from concourse.bass_utils import run_bass_kernel_spmd

F32 = mybir.dt.float32
BF16 = mybir.dt.bfloat16
AF = mybir.ActivationFunctionType
ALU = mybir.AluOpType
AX = mybir.AxisListType
NPBF = ml_dtypes.bfloat16

B, S, D = 4, 4096, 1024
NB, NT = 32, 16
TPC = NT * 128
H, DH = 16, 64
NKT = 32
BIG = 1.0e30
MAGIC = float(3 << 22)

N_SUB = 8


def blocks_for(half):
    return [j for j in range(NB) if (j % 4 in ((0, 3) if half == 0 else (1, 2)))]


_slot = {}
for _r in (0, 1):
    for _s, _j in enumerate(blocks_for(_r)):
        _slot[_j] = (_r, _s)


def build_program(n_sub=None):
    n_sub = N_SUB if n_sub is None else n_sub
    nc = bass.Bass("TRN2", target_bir_lowering=False, debug=False)

    def inp(name, shape, dtype=F32):
        return nc.dram_tensor(name, list(shape), dtype, kind="ExternalInput").ap()

    x0 = inp("x0", [TPC, D])
    ident_f = inp("ident", [128, 128])
    cosrep = inp("cosrep", [TPC, 512])
    qidxb = inp("qidxb", [128, TPC])
    wI = {}
    for nm, shp in [
        ("l0aa_win", [D, 3 * D]), ("l0aa_bin", [1, 3 * D]),
        ("l0aa_wout", [D, D]), ("l0aa_bout", [1, D]),
        ("l0aa_gamma", [1, D]), ("l0aa_beta", [1, D]),
        ("l0ab_wembT", [D, 2 * D]), ("l0ab_wanchT", [DH, 64]),
        ("l0ab_woutT", [D, D]), ("l0ab_bout", [1, D]),
        ("l0f_wgT", [D, 4096]), ("l0f_wuT", [D, 4096]), ("l0f_wdT", [4096, D]),
        ("l1aa_wqT", [D, D]), ("l1aa_wkT", [D, D]), ("l1aa_wvT", [D, D]),
        ("l1ab_wq", [D, D]), ("l1ab_wk", [D, D]), ("l1ab_wv", [D, D]),
        ("l1ab_bq", [1, D]), ("l1ab_bk", [1, D]), ("l1ab_bv", [1, D]),
        ("l1ab_gamma", [1, D]), ("l1ab_beta", [1, D]),
        ("l1f_wg", [2816, D]), ("l1f_wu", [2816, D]), ("l1f_wd", [D, 2816]),
    ]:
        wI[nm] = inp(nm, shp)

    y = nc.dram_tensor("y", [TPC, D], F32, kind="ExternalOutput").ap()

    xd = nc.dram_tensor("xd", [TPC, D], F32).ap()
    hd = nc.dram_tensor("hd", [TPC, D], F32).ap()
    hrows = nc.dram_tensor("hrows", [TPC, 4096], F32).ap()
    EXN = D * TPC + TPC * D + 2048
    exin = [nc.dram_tensor(f"ex{i}in", [EXN], BF16).ap() for i in range(3)]
    exout = [nc.dram_tensor(f"ex{i}out", [2 * EXN], BF16).ap() for i in range(3)]
    ABN = 64 * 16 * 68
    abin = nc.dram_tensor("abin", [64, 16 * 68], F32).ap()
    about = nc.dram_tensor("about", [64, 16 * 68], F32).ap()
    wq0 = {k: nc.dram_tensor(f"wq0{k}", shp, BF16).ap() for k, shp in
           [("g", [D, 4096]), ("u", [D, 4096]), ("d", [4096, D])]}
    wq1 = {k: nc.dram_tensor(f"wq1{k}", shp, BF16).ap() for k, shp in
           [("g", [2816, D]), ("u", [2816, D]), ("d", [D, 2816])]}

    ctxs = []

    def ent(cm):
        ctxs.append(cm)
        return cm.__enter__()

    ab = ent(nc.sbuf_tensor("ab", [128, 72000], BF16))
    af = ent(nc.sbuf_tensor("af", [128, 11100], F32))
    ident_bf = ent(nc.sbuf_tensor("ident_bf", [128, 128], BF16))
    ident32 = ent(nc.sbuf_tensor("ident32", [128, 128], F32))
    onesb = ent(nc.sbuf_tensor("onesb", [128, 128], BF16))
    ones32 = ent(nc.sbuf_tensor("ones32", [128, 128], F32))
    util = ent(nc.sbuf_tensor("util", [128, 1400], F32))
    urow = ent(nc.sbuf_tensor("urow", [128, 3200], F32))

    psF = ent(nc.psum_tensor("psF", [128, 3072], F32))
    psB = ent(nc.psum_tensor("psB", [128, 2048], BF16))

    QS = ent(nc.semaphore("qs"))
    qc = [0]

    blk = ent(nc.Block())
    T, V, Sc, G, SY = nc.tensor, nc.vector, nc.scalar, nc.gpsimd, nc.sync
    PROG = []

    def q(eng, fn, dma=False):
        n = 16 if dma else 1
        PROG.append((id(eng), fn, qc[0], n))
        qc[0] += n

    def qb(eng, fns, dma=False):
        n = 16 if dma else 1
        w = qc[0]
        for fn in fns:
            PROG.append((id(eng), fn, w, n))
            qc[0] += n

    XT = af[:, 0:1024]
    HT = af[:, 1024:2048]
    T1 = af[:, 2048:3072]
    T2 = af[:, 3072:4096]
    WF = af[:, 4096:8192]
    F2 = af[:, 8192:11008]

    LNS = util[:, 0:32]
    MC = util[:, 32:40]
    KMC = util[:, 40:56]
    KMB = util[:, 56:88]
    IOT = util[:, 88:120]
    ISC = util[:, 120:136]
    ISC2 = util[:, 136:152]
    BM = util[:, 376:552]
    RS = util[:, 552:728]
    GCOL = util[:, 888:896]
    BCOL = util[:, 896:904]
    SWC = util[:, 904:912]
    ISW = util[:, 912:920]
    MRC = util[:, 920:928]
    ZC = util[:, 928:930]

    q(G, lambda: G.dma_start(out=ident_bf[:], in_=ident_f), dma=True)
    q(SY, lambda: SY.dma_start(out=ident32[:], in_=ident_f), dma=True)
    q(V, lambda: V.memset(onesb[:], 1.0))
    q(V, lambda: V.memset(ones32[:], 1.0))
    q(V, lambda: V.memset(ZC, 0.0))
    q(G, lambda: G.iota(IOT, pattern=[[128, 32]], base=0, channel_multiplier=1,
                        allow_small_or_imprecise_dtypes=True))
    q(SY, lambda: SY.dma_start(out=xd, in_=x0), dma=True)

    def ln_tile(xt, out, eps=1e-5, km_col=None):
        W = xt.shape[-1]
        ns = W // 512
        for i in range(ns):
            q(V, lambda i=i: V.bn_stats(LNS[:, 6 * i:6 * i + 6],
                                        xt[:, 512 * i:512 * (i + 1)]))
        q(V, lambda ns=ns: V.bn_aggr(LNS[:, 24:26], LNS[:, 0:6 * ns]))
        mcol, rcol = MC[:, 0:1], MC[:, 1:2]
        q(V, lambda: V.tensor_copy(mcol, LNS[:, 24:25]))
        q(V, lambda: V.tensor_scalar(rcol, LNS[:, 25:26], float(eps), None, ALU.add))
        q(V, lambda: V.reciprocal(rcol, rcol))
        q(Sc, lambda: Sc.activation(rcol, rcol, AF.Sqrt))
        if km_col is not None:
            q(V, lambda: V.reduce_sum(km_col, xt, axis=AX.X))
            q(V, lambda: V.tensor_scalar(km_col, km_col, 0.0, None, ALU.not_equal))
        q(V, lambda: V.tensor_scalar(out, xt, mcol, rcol, ALU.subtract, ALU.mult))

    def tm2fm(tile_fn, fm, width, ntiles=NT, fm_tok=TPC):
        for c in range(width // 128):
            for t0 in range(0, ntiles, 4):
                kk = min(4, ntiles - t0)
                for k in range(kk):
                    tl = tile_fn(t0 + k)
                    q(T, lambda c=c, k=k, tl=tl: T.transpose(
                        psB[:, k * 128:(k + 1) * 128],
                        tl[:, c * 128:(c + 1) * 128], ident_bf[:]))
                q(V, lambda c=c, t0=t0, kk=kk: V.tensor_copy(
                    fm[:, c * fm_tok + t0 * 128:c * fm_tok + (t0 + kk) * 128],
                    psB[:, 0:kk * 128]))

    def prelude(dst_bf_fn, src=None, park_h=False, double_ln=False,
                km_cols=None, quant=None):
        src = xd if src is None else src
        for t in range(NT):
            q(SY, lambda t=t: SY.dma_start(out=XT,
                                           in_=src[t * 128:(t + 1) * 128, :]),
              dma=True)
            km = km_cols[:, t:t + 1] if km_cols is not None else None
            ln_tile(XT, HT, km_col=km)
            if park_h:
                q(SY, lambda t=t: SY.dma_start(out=hd[t * 128:(t + 1) * 128, :],
                                               in_=HT), dma=True)
            if double_ln:
                ln_tile(HT, HT)
            if quant is None:
                q(V, lambda t=t: V.tensor_copy(dst_bf_fn(t), HT))
            else:
                quant(t, HT, dst_bf_fn(t))

    def bitnet_xq(t, src, dst, isc):
        mx = MRC[:, 0:1]
        q(V, lambda: V.reduce_max(mx, src, axis=AX.X, apply_absolute_value=True))
        q(V, lambda: V.tensor_scalar_max(mx, mx, 1e-5))
        sx = MRC[:, 1:2]
        q(V, lambda: V.reciprocal(sx, mx))
        q(V, lambda: V.tensor_scalar_mul(sx, sx, 127.0))
        q(V, lambda t=t: V.tensor_scalar_mul(isc[:, t:t + 1], mx, 1.0 / 127.0))
        q(V, lambda: V.tensor_scalar(T1, src, sx, MAGIC, ALU.mult, ALU.add))
        q(V, lambda dst=dst: V.tensor_scalar(dst, T1, MAGIC, None, ALU.subtract))

    def fp4_qd(src, dst, W, fA, fB, tmpbase):
        nb = W // 16
        bm, rs = BM[:, 0:nb], RS[:, 0:nb]
        q(V, lambda: V.reduce_max(bm, src.rearrange("p (b s) -> p b s", s=16),
                                  axis=AX.X, apply_absolute_value=True))
        q(V, lambda: V.tensor_scalar(bm, bm, 1.0 / 6.0, 1e-12, ALU.mult, ALU.max))
        q(V, lambda: V.reciprocal(rs, bm))
        rse = fA
        qb(V, [(lambda k=k: V.tensor_copy(
            rse.rearrange("p (b s) -> p b s", s=16)[:, :, k:k + 1], rs))
            for k in range(16)])
        v = fB
        q(V, lambda: V.tensor_mul(v, src, rse))
        r1 = ab[:, tmpbase:tmpbase + W]
        r2 = ab[:, tmpbase + W:tmpbase + 2 * W]
        acc = ab[:, tmpbase + 2 * W:tmpbase + 3 * W]
        cmp_ = ab[:, tmpbase + 3 * W:tmpbase + 4 * W]
        tf = src
        q(V, lambda: V.tensor_scalar(tf, v, 2.0, MAGIC, ALU.mult, ALU.add))
        q(V, lambda: V.tensor_scalar(r1, tf, MAGIC, 0.5, ALU.subtract, ALU.mult))
        q(V, lambda: V.tensor_copy(acc, r1))
        q(V, lambda: V.tensor_scalar(tf, v, MAGIC, None, ALU.add))
        q(V, lambda: V.tensor_scalar(r2, tf, MAGIC, None, ALU.subtract))
        q(V, lambda: V.tensor_sub(r2, r2, r1))
        q(Sc, lambda: Sc.activation(tf, v, AF.Abs))
        q(V, lambda: V.tensor_scalar(cmp_, tf, 2.0, None, ALU.is_gt))
        q(V, lambda: V.tensor_mul(r2, r2, cmp_))
        q(V, lambda: V.tensor_add(acc, acc, r2))
        q(V, lambda: V.tensor_scalar(tf, v, MAGIC, None, ALU.add))
        q(V, lambda: V.tensor_scalar(r2, tf, MAGIC, None, ALU.subtract))
        q(V, lambda: V.tensor_scalar(tf, v, 0.5, MAGIC, ALU.mult, ALU.add))
        q(V, lambda: V.tensor_scalar(r1, tf, MAGIC, 2.0, ALU.subtract, ALU.mult))
        q(V, lambda: V.tensor_sub(r1, r1, r2))
        q(Sc, lambda: Sc.activation(tf, v, AF.Abs))
        q(V, lambda: V.tensor_scalar(cmp_, tf, 4.0, None, ALU.is_gt))
        q(V, lambda: V.tensor_mul(r1, r1, cmp_))
        q(V, lambda: V.tensor_add(acc, acc, r1))
        q(V, lambda: V.reciprocal(rse, rse))
        q(V, lambda: V.tensor_copy(fB, acc))
        q(V, lambda: V.tensor_mul(fB, fB, rse))
        q(V, lambda: V.tensor_copy(dst, fB))

    def exchange(i):
        q(G, lambda i=i: G.collective_compute(
            "AllGather", ALU.bypass,
            replica_groups=[[0, 1], [2, 3], [4, 5], [6, 7]],
            ins=[exin[i].opt()], outs=[exout[i].opt()]))

    def unpack_kts(i, c, kts):
        qb(SY, [(lambda r=r: SY.dma_start(
            out=kts[:, r * 2048:(r + 1) * 2048],
            in_=exout[i][r * EXN + c * 128 * TPC:r * EXN + (c + 1) * 128 * TPC]
            .rearrange("(p t) -> p t", p=128))) for r in (0, 1)], dma=True)

    def unpack_vs(i, h0, nh, vs, stride):
        for r in (0, 1):
            for hh in range(nh):
                q(SY, lambda r=r, hh=hh: SY.dma_start(
                    out=vs.rearrange("p (k c) -> p k c", c=stride)
                    [:, 16 * r:16 * (r + 1), 65 * hh:65 * hh + 64],
                    in_=exout[i][r * EXN + D * TPC:r * EXN + 2 * D * TPC]
                    .rearrange("(t p c) -> p t c", p=128, c=D)
                    [:, :, 64 * (h0 + hh):64 * (h0 + hh) + 64]), dma=True)
        q(V, lambda: V.memset(
            vs.rearrange("p (k h c) -> p k h c", h=nh, c=65)[:, :, :, 64:65], 1.0))

    def cast_w_stream(dst, srcap, rows, cols, scale=None, gcol=None):
        for c in range(rows // 128):
            if scale is None and gcol is None:
                q(G, lambda c=c: G.dma_start(
                    out=dst[:, c * cols:(c + 1) * cols],
                    in_=srcap[c * 128:(c + 1) * 128, :]), dma=True)
            else:
                q(SY, lambda c=c: SY.dma_start(
                    out=WF[:, 0:cols], in_=srcap[c * 128:(c + 1) * 128, :]),
                  dma=True)
                s1 = gcol[:, c:c + 1] if gcol is not None else (scale or 1.0)
                if gcol is not None:
                    q(V, lambda c=c, s1=s1: V.tensor_scalar(
                        dst[:, c * cols:(c + 1) * cols], WF[:, 0:cols], s1, None,
                        ALU.mult))
                else:
                    q(V, lambda c=c: V.tensor_scalar(
                        dst[:, c * cols:(c + 1) * cols], WF[:, 0:cols],
                        float(scale), None, ALU.mult))

    def bias_row(dst_bf, srcap, W, scale_slice=None):
        r = urow[0:1, 0:W]
        q(SY, lambda: SY.dma_start(out=r, in_=srcap), dma=True)
        if scale_slice is not None:
            a, b = scale_slice
            q(V, lambda: V.tensor_scalar_mul(r[:, a:b], r[:, a:b], 0.125))
        q(V, lambda: V.tensor_copy(dst_bf, r))

    def residual_out(t, psap, hadd=None, scale_col=None, dst=None):
        dst = xd if dst is None else dst
        q(SY, lambda t=t: SY.dma_start(out=XT, in_=xd[t * 128:(t + 1) * 128, :]),
          dma=True)
        if scale_col is not None:
            q(V, lambda: V.tensor_scalar(T2, psap, scale_col, None, ALU.mult))
        else:
            q(V, lambda: V.tensor_copy(T2, psap))
        if hadd is not None:
            q(SY, lambda t=t: SY.dma_start(out=HT, in_=hd[t * 128:(t + 1) * 128, :]),
              dma=True)
            q(V, lambda: V.tensor_add(T2, T2, HT))
        q(V, lambda: V.tensor_add(T2, T2, XT))
        q(SY, lambda t=t, dst=dst: SY.dma_start(
            out=dst[t * 128:(t + 1) * 128, :], in_=T2), dma=True)

    def s1_l0aa():
        FM1 = ab[:, 0:16384]
        QFM = ab[:, 16384:32768]
        WIN = ab[:, 32768:57344]
        BRW = ab[:, 57344:60416]
        STG = ab[:, 60416:62464]
        KPC = ab[:, 62464:66560]

        def xn_tile(t):
            return QFM[:, t * 1024:(t + 1) * 1024]

        prelude(xn_tile, src=xd, park_h=True, double_ln=True)
        tm2fm(xn_tile, FM1, 1024)

        q(SY, lambda: SY.dma_start(
            out=GCOL, in_=wI["l0aa_gamma"].rearrange("o (c p) -> p c", p=128),
            allow_slow_non_contiguous=True), dma=True)
        q(SY, lambda: SY.dma_start(
            out=BCOL, in_=wI["l0aa_beta"].rearrange("o (c p) -> p c", p=128),
            allow_slow_non_contiguous=True), dma=True)
        for c in range(8):
            q(SY, lambda c=c: SY.dma_start(
                out=WF[:, 0:3072], in_=wI["l0aa_win"][c * 128:(c + 1) * 128, :]),
              dma=True)
            for bk in range(6):
                q(T, lambda c=c, bk=bk: T.matmul(
                    psF[0:1, bk * 512:(bk + 1) * 512], BCOL[:, c:c + 1],
                    WF[:, bk * 512:(bk + 1) * 512], start=(c == 0), stop=(c == 7)))
            q(V, lambda c=c: V.tensor_scalar(
                WIN[:, c * 3072:c * 3072 + 1024], WF[:, 0:1024],
                GCOL[:, c:c + 1], None, ALU.mult))
            q(V, lambda c=c: V.tensor_scalar(
                WIN[:, c * 3072 + 1024:c * 3072 + 2048], WF[:, 1024:2048],
                GCOL[:, c:c + 1], 0.125, ALU.mult, ALU.mult))
            q(V, lambda c=c: V.tensor_scalar(
                WIN[:, c * 3072 + 2048:c * 3072 + 3072], WF[:, 2048:3072],
                GCOL[:, c:c + 1], None, ALU.mult))
        br = urow[0:1, 0:3072]
        q(V, lambda: V.tensor_copy(br, psF[0:1, 0:3072]))
        bi = WF[0:1, 0:3072]
        q(SY, lambda: SY.dma_start(out=bi, in_=wI["l0aa_bin"]), dma=True)
        q(V, lambda: V.tensor_add(br, br, bi))
        q(V, lambda: V.tensor_scalar_mul(br[:, 1024:2048], br[:, 1024:2048], 0.125))
        q(V, lambda: V.tensor_copy(BRW[0:1, 0:3072], br))

        vbf = STG[:, 0:1024]
        qtm = STG[:, 0:1024]
        ktm = STG[:, 1024:2048]
        for t in range(NT):
            for din in range(8):
                for bk in range(6):
                    q(T, lambda t=t, din=din, bk=bk: T.matmul(
                        psF[:, bk * 512:(bk + 1) * 512],
                        FM1[:, din * TPC + t * 128:din * TPC + (t + 1) * 128],
                        WIN[:, din * 3072 + bk * 512:din * 3072 + (bk + 1) * 512],
                        start=(din == 0), stop=False))
            for bk in range(6):
                q(T, lambda bk=bk: T.matmul(
                    psF[:, bk * 512:(bk + 1) * 512], onesb[0:1, 0:128],
                    BRW[0:1, bk * 512:(bk + 1) * 512], start=False, stop=True))
            q(V, lambda: V.tensor_copy(vbf, psF[:, 2048:3072]))
            q(SY, lambda t=t: SY.dma_start(
                out=exin[0][D * TPC + t * 128 * D:D * TPC + (t + 1) * 128 * D]
                .rearrange("(p c) -> p c", p=128), in_=vbf), dma=True)
            q(V, lambda: V.tensor_copy(XT, psF[:, 0:1024]))
            q(V, lambda: V.tensor_copy(HT, psF[:, 1024:2048]))
            q(SY, lambda t=t: SY.dma_start(out=WF[:, 0:512],
                                           in_=cosrep[t * 128:(t + 1) * 128, :]),
              dma=True)
            for src32 in (XT, HT):
                hv = src32.rearrange("p (h x) -> p h x", h=16)
                rp = hv[:, :, 0:32].rearrange("p h (j u) -> p h j u", u=2)
                ev, od = rp[:, :, :, 0:1], rp[:, :, :, 1:2]
                cosv, sinv = WF[:, 0:256], WF[:, 256:512]
                q(V, lambda ev=ev: V.tensor_mul(T1[:, 0:256], ev, cosv))
                q(V, lambda od=od: V.tensor_mul(T1[:, 256:512], od, sinv))
                q(V, lambda od=od: V.tensor_mul(T2[:, 0:256], od, cosv))
                q(V, lambda ev=ev: V.tensor_mul(T2[:, 256:512], ev, sinv))
                q(V, lambda ev=ev: V.tensor_sub(ev, T1[:, 0:256], T1[:, 256:512]))
                q(V, lambda od=od: V.tensor_add(od, T2[:, 0:256], T2[:, 256:512]))
            q(V, lambda: V.tensor_copy(qtm, XT))
            q(V, lambda: V.tensor_copy(ktm, HT))
            for c in range(8):
                q(T, lambda c=c: T.transpose(psB[:, 0:128], qtm[:, c * 128:(c + 1) * 128],
                                             ident_bf[:]))
                q(V, lambda c=c, t=t: V.tensor_copy(
                    QFM[:, c * TPC + t * 128:c * TPC + (t + 1) * 128], psB[:, 0:128]))
            for c in range(8):
                q(T, lambda c=c: T.transpose(psB[:, 0:128], ktm[:, c * 128:(c + 1) * 128],
                                             ident_bf[:]))
                q(V, lambda c=c, t=t: V.tensor_copy(
                    KPC[:, c * 512 + (t % 4) * 128:c * 512 + (t % 4 + 1) * 128],
                    psB[:, 0:128]))
            if t % 4 == 3:
                t0 = t - 3
                qb(SY, [(lambda c=c, t0=t0: SY.dma_start(
                    out=exin[0][c * 128 * TPC:(c + 1) * 128 * TPC]
                    .rearrange("(p t) -> p t", p=128)[:, t0 * 128:(t0 + 4) * 128],
                    in_=KPC[:, c * 512:(c + 1) * 512])) for c in range(8)],
                    dma=True)

        exchange(0)

        KTS = ab[:, 49152:53248]
        VS = ab[:, 53248:57408]
        ATB = ab[:, 57408:58944]
        CEV = ab[:, 58944:59008]
        CTX = ab[:, 32768:49152]
        for hg in range(8):
            unpack_kts(0, hg, KTS)
            unpack_vs(0, hg * 2, 2, VS, 130)
            for hh in range(2):
                pb = 64 * hh
                hcol = hg * 2048
                for sp in range(4):
                    qs0 = hcol + sp * 512
                    for ktj in range(NKT):
                        r, sl = _slot[ktj]
                        kcol = r * 2048 + sl * 128
                        vcol = (16 * r + sl) * 130 + 65 * hh
                        pss = psF[:, (ktj % 2) * 512:(ktj % 2) * 512 + 512]
                        atb = ATB[:, (ktj % 3) * 512:(ktj % 3) * 512 + 512]
                        q(T, lambda pss=pss, kcol=kcol, qs0=qs0, pb=pb: T.matmul(
                            pss, KTS[pb:pb + 64, kcol:kcol + 128],
                            QFM[pb:pb + 64, qs0:qs0 + 512], start=True, stop=True))
                        q(Sc, lambda atb=atb, pss=pss: Sc.activation(
                            atb, pss, AF.Exp))
                        q(T, lambda vcol=vcol, atb=atb, ktj=ktj: T.matmul(
                            psF[0:65, 1024:1536], VS[:, vcol:vcol + 65], atb,
                            start=(ktj == 0), stop=(ktj == NKT - 1)))
                    q(V, lambda: V.tensor_copy(ab[0:65, 58944:58944 + 512],
                                               psF[0:65, 1024:1536]))
                    for qt in range(4):
                        q(T, lambda qt=qt: T.transpose(
                            psB[:, 0:65],
                            ab[0:65, 58944 + qt * 128:58944 + (qt + 1) * 128],
                            ident_bf[0:65, 0:65]))
                        q(V, lambda: V.tensor_copy(T1[:, 0:65], psB[:, 0:65]))
                        den = MRC[:, 2:3]
                        q(V, lambda: V.tensor_scalar(den, T1[:, 64:65], 1e-38, None,
                                                     ALU.max))
                        q(V, lambda: V.reciprocal(den, den))
                        tt = sp * 4 + qt
                        q(V, lambda tt=tt, hg=hg, hh=hh, den=den: V.tensor_scalar(
                            CTX[:, tt * 1024 + (hg * 2 + hh) * 64:
                                tt * 1024 + (hg * 2 + hh) * 64 + 64],
                            T1[:, 0:64], den, None, ALU.mult))

        CFM = ab[:, 16384:32768]
        WOT = ab[:, 0:8192]
        BO = ab[:, 8192:9216]

        def ctx_tile(t):
            return CTX[:, t * 1024:(t + 1) * 1024]

        tm2fm(ctx_tile, CFM, 1024)
        cast_w_stream(WOT, wI["l0aa_wout"], D, D)
        bias_row(BO[0:1, 0:1024], wI["l0aa_bout"], 1024)
        for t in range(NT):
            for din in range(8):
                for bk in range(2):
                    q(T, lambda t=t, din=din, bk=bk: T.matmul(
                        psF[:, bk * 512:(bk + 1) * 512],
                        CFM[:, din * TPC + t * 128:din * TPC + (t + 1) * 128],
                        WOT[:, din * 1024 + bk * 512:din * 1024 + (bk + 1) * 512],
                        start=(din == 0), stop=False))
            for bk in range(2):
                q(T, lambda bk=bk: T.matmul(
                    psF[:, bk * 512:(bk + 1) * 512], onesb[0:1, 0:128],
                    BO[0:1, bk * 512:(bk + 1) * 512], start=False, stop=True))
            residual_out(t, psF[:, 0:1024], hadd=True)

    def s_ffn0(first):
        XQT = ab[:, 0:16384]
        XQF = ab[:, 16384:32768]
        WBF = ab[:, 32768:37120]
        WGS = ab[:, 37120:41216]
        WUS = ab[:, 41216:45312]

        def xq_tile(t):
            return XQT[:, t * 1024:(t + 1) * 1024]

        prelude(xq_tile, src=xd,
                quant=lambda t, src, dst: bitnet_xq(t, src, dst, ISC))
        tm2fm(xq_tile, XQF, 1024)

        if first:
            for wi, (nm, ap_, rows, cols) in enumerate([
                    ("g", wI["l0f_wgT"], D, 4096), ("u", wI["l0f_wuT"], D, 4096),
                    ("d", wI["l0f_wdT"], 4096, D)]):
                acc = MRC[:, 3:4]
                q(V, lambda: V.memset(acc, 0.0))
                nch = rows // 128
                for c in range(nch):
                    q(SY, lambda c=c, ap_=ap_, cols=cols: SY.dma_start(
                        out=WF[:, 0:cols], in_=ap_[c * 128:(c + 1) * 128, :]),
                      dma=True)
                    q(V, lambda cols=cols: V.reduce_sum(
                        MRC[:, 4:5], WF[:, 0:cols], axis=AX.X,
                        apply_absolute_value=True))
                    q(V, lambda: V.tensor_add(acc, acc, MRC[:, 4:5]))
                q(T, lambda: T.transpose(psF[0:1, 0:128], acc, ident32[:]))
                q(V, lambda: V.tensor_copy(urow[0:1, 0:128], psF[0:1, 0:128]))
                q(V, lambda: V.reduce_sum(urow[0:1, 128:129], urow[0:1, 0:128],
                                          axis=AX.X))
                q(V, lambda rows=rows, cols=cols: V.tensor_scalar(
                    urow[0:1, 129:130], urow[0:1, 128:129],
                    1.0 / float(rows * cols), 1e-5, ALU.mult, ALU.max))
                q(V, lambda: V.reciprocal(urow[0:1, 130:131], urow[0:1, 129:130]))
                q(T, lambda: T.matmul(psF[:, 0:2], ones32[0:1, 0:128],
                                      urow[0:1, 129:131], start=True, stop=True))
                q(V, lambda wi=wi: V.tensor_copy(ISW[:, wi:wi + 1], psF[:, 0:1]))
                q(V, lambda wi=wi: V.tensor_copy(SWC[:, wi:wi + 1], psF[:, 1:2]))
                for c in range(nch):
                    q(SY, lambda c=c, ap_=ap_, cols=cols: SY.dma_start(
                        out=WF[:, 0:cols], in_=ap_[c * 128:(c + 1) * 128, :]),
                      dma=True)
                    q(V, lambda cols=cols, wi=wi: V.tensor_scalar(
                        WF[:, 0:cols], WF[:, 0:cols], SWC[:, wi:wi + 1], MAGIC,
                        ALU.mult, ALU.add))
                    q(V, lambda cols=cols: V.tensor_scalar(
                        WF[:, 0:cols], WF[:, 0:cols], MAGIC, 1.0,
                        ALU.subtract, ALU.min))
                    q(V, lambda cols=cols: V.tensor_scalar(
                        WBF[:, 0:cols], WF[:, 0:cols], -1.0, None, ALU.max))
                    q(SY, lambda c=c, nm=nm, cols=cols: SY.dma_start(
                        out=wq0[nm][c * 128:(c + 1) * 128, :], in_=WBF[:, 0:cols]),
                      dma=True)

        gsc = util[:, 152:168]
        usc = util[:, 168:184]
        dscc = util[:, 184:200]
        qb(V, [(lambda t=t: V.tensor_scalar(gsc[:, t:t + 1], ISC[:, t:t + 1],
                                            ISW[:, 0:1], None, ALU.mult))
               for t in range(NT)])
        qb(V, [(lambda t=t: V.tensor_scalar(usc[:, t:t + 1], ISC[:, t:t + 1],
                                            ISW[:, 1:2], None, ALU.mult))
               for t in range(NT)])

        for ic in range(8):
            qb(SY, [(lambda c=c, ic=ic: SY.dma_start(
                out=WGS[:, c * 512:(c + 1) * 512],
                in_=wq0["g"][c * 128:(c + 1) * 128, ic * 512:(ic + 1) * 512]))
                for c in range(8)], dma=True)
            qb(SY, [(lambda c=c, ic=ic: SY.dma_start(
                out=WUS[:, c * 512:(c + 1) * 512],
                in_=wq0["u"][c * 128:(c + 1) * 128, ic * 512:(ic + 1) * 512]))
                for c in range(8)], dma=True)
            for t in range(NT):
                for din in range(8):
                    q(T, lambda t=t, din=din: T.matmul(
                        psF[:, 0:512],
                        XQF[:, din * TPC + t * 128:din * TPC + (t + 1) * 128],
                        WGS[:, din * 512:(din + 1) * 512],
                        start=(din == 0), stop=(din == 7)))
                    q(T, lambda t=t, din=din: T.matmul(
                        psF[:, 512:1024],
                        XQF[:, din * TPC + t * 128:din * TPC + (t + 1) * 128],
                        WUS[:, din * 512:(din + 1) * 512],
                        start=(din == 0), stop=(din == 7)))
                q(V, lambda t=t: V.tensor_scalar(T1[:, 0:512], psF[:, 0:512],
                                                 gsc[:, t:t + 1], None, ALU.mult))
                q(Sc, lambda: Sc.activation(T1[:, 512:1024], T1[:, 0:512], AF.Silu))
                q(V, lambda t=t: V.scalar_tensor_tensor(
                    T2[:, 0:512], psF[:, 512:1024], usc[:, t:t + 1],
                    T1[:, 512:1024], ALU.mult, ALU.mult))
                q(SY, lambda t=t, ic=ic: SY.dma_start(
                    out=hrows[t * 128:(t + 1) * 128, ic * 512:(ic + 1) * 512],
                    in_=T2[:, 0:512]), dma=True)

        WDS = ab[:, 0:32768]
        HQ = ab[:, 32768:36864]
        HQF = ab[:, 36864:40960]
        for c in range(32):
            q(G, lambda c=c: G.dma_start(
                out=WDS[:, c * 1024:(c + 1) * 1024],
                in_=wq0["d"][c * 128:(c + 1) * 128, :]), dma=True)
        for t in range(NT):
            q(SY, lambda t=t: SY.dma_start(out=WF,
                                           in_=hrows[t * 128:(t + 1) * 128, :]),
              dma=True)
            mx = MRC[:, 0:1]
            q(V, lambda: V.reduce_max(mx, WF, axis=AX.X, apply_absolute_value=True))
            q(V, lambda: V.tensor_scalar_max(mx, mx, 1e-5))
            sx = MRC[:, 1:2]
            q(V, lambda: V.reciprocal(sx, mx))
            q(V, lambda: V.tensor_scalar_mul(sx, sx, 127.0))
            hsc = MRC[:, 5:6]
            q(V, lambda: V.tensor_scalar(hsc, mx, 1.0 / 127.0, None, ALU.mult))
            q(V, lambda: V.tensor_scalar(hsc, hsc, ISW[:, 2:3], None, ALU.mult))
            q(V, lambda: V.tensor_scalar(WF, WF, sx, MAGIC, ALU.mult, ALU.add))
            q(V, lambda: V.tensor_scalar(HQ, WF, MAGIC, None, ALU.subtract))
            for c0 in range(0, 32, 4):
                for k in range(4):
                    q(T, lambda c0=c0, k=k: T.transpose(
                        psB[:, k * 128:(k + 1) * 128],
                        HQ[:, (c0 + k) * 128:(c0 + k + 1) * 128], ident_bf[:]))
                q(V, lambda c0=c0: V.tensor_copy(
                    HQF[:, c0 * 128:(c0 + 4) * 128], psB[:, 0:512]))
            for din in range(32):
                for bk in range(2):
                    q(T, lambda t=t, din=din, bk=bk: T.matmul(
                        psF[:, bk * 512:(bk + 1) * 512],
                        HQF[:, din * 128:(din + 1) * 128],
                        WDS[:, din * 1024 + bk * 512:din * 1024 + (bk + 1) * 512],
                        start=(din == 0), stop=(din == 31)))
            residual_out(t, psF[:, 0:1024], scale_col=hsc)

    def s3_l0ab():
        XNT = ab[:, 0:16384]
        FM1 = ab[:, 16384:32768]
        WEB = ab[:, 32768:49152]
        VSL = ab[:, 49152:65792]
        WAN = ab[:, 65792:65856]
        OUT2 = ab[:, 65856:66880]

        def xn_tile(t):
            return XNT[:, t * 1024:(t + 1) * 1024]

        prelude(xn_tile, src=xd)
        tm2fm(xn_tile, FM1, 1024)
        cast_w_stream(WEB, wI["l0ab_wembT"], D, 2048)
        q(SY, lambda: SY.dma_start(out=WF[0:64, 0:64], in_=wI["l0ab_wanchT"]),
          dma=True)
        q(SY, lambda: SY.dma_start(out=WF[64:128, 0:64], in_=wI["l0ab_wanchT"]),
          dma=True)
        q(V, lambda: V.tensor_scalar(ab[:, 65792:65856], WF[:, 0:64],
                                     0.125, None, ALU.mult))

        QKF = ab[:, 0:16384]
        for dc in range(8):
            for sp in range(4):
                bk = sp % 2
                for din in range(8):
                    q(T, lambda dc=dc, sp=sp, bk=bk, din=din: T.matmul(
                        psF[:, bk * 512:(bk + 1) * 512],
                        WEB[:, din * 2048 + dc * 128:din * 2048 + (dc + 1) * 128],
                        FM1[:, din * TPC + sp * 512:din * TPC + (sp + 1) * 512],
                        start=(din == 0), stop=(din == 7)))
                q(V, lambda dc=dc, sp=sp, bk=bk: V.tensor_copy(
                    QKF[:, dc * 2048 + sp * 512:dc * 2048 + (sp + 1) * 512],
                    psF[:, bk * 512:(bk + 1) * 512]))
        q(V, lambda: V.memset(
            VSL.rearrange("p (t h c) -> p t h c", h=16, c=65)[:, :, :, 64:65], 1.0))
        for t in range(NT):
            for din in range(8):
                for bk in range(2):
                    q(T, lambda t=t, din=din, bk=bk: T.matmul(
                        psF[:, bk * 512:(bk + 1) * 512],
                        FM1[:, din * TPC + t * 128:din * TPC + (t + 1) * 128],
                        WEB[:, din * 2048 + 1024 + bk * 512:
                            din * 2048 + 1024 + (bk + 1) * 512],
                        start=(din == 0), stop=(din == 7)))
            q(V, lambda t=t: V.tensor_copy(
                VSL[:, t * 1040:(t + 1) * 1040]
                .rearrange("p (h c) -> p h c", c=65)[:, :, 0:64],
                psF[:, 0:1024].rearrange("p (h c) -> p h c", c=64)))

        ATT = ab[:, 16384:32768]
        for t in range(NT):
            for h in range(16):
                q(T, lambda t=t, h=h: T.matmul(
                    psF[:, 64 * h:64 * h + 64],
                    QKF[64 * (h % 2):64 * (h % 2) + 64,
                        (h // 2) * 2048 + t * 128:(h // 2) * 2048 + (t + 1) * 128],
                    ab[64 * (h % 2):64 * (h % 2) + 64, 65792:65856],
                    start=True, stop=True, skip_group_check=True))
            nmx = MRC[:, 0:1]
            q(V, lambda: V.reduce_max(nmx, psF[:, 0:1024], axis=AX.X, negate=True))
            q(Sc, lambda: Sc.activation(F2[:, 0:1024], psF[:, 0:1024], AF.Exp,
                                        bias=nmx))
            sc_ = util[:, 200:216]
            q(V, lambda: V.reduce_sum(
                sc_, F2[:, 0:1024].rearrange("p (h c) -> p h c", h=16), axis=AX.X))
            q(V, lambda: V.reciprocal(sc_, sc_))
            for h in range(16):
                q(V, lambda t=t, h=h: V.tensor_scalar(
                    ATT[:, t * 1024 + 64 * h:t * 1024 + 64 * h + 64],
                    F2[:, 64 * h:64 * h + 64], sc_[:, h:h + 1], None, ALU.mult))
        for t in range(NT):
            for h in range(16):
                slot = 1024 + (h // 4) * 512 + (h % 4) * 68
                q(T, lambda t=t, h=h, slot=slot: T.matmul(
                    psF[0:64, slot:slot + 65],
                    ATT[:, t * 1024 + 64 * h:t * 1024 + 64 * h + 64],
                    VSL[:, t * 1040 + 65 * h:t * 1040 + 65 * (h + 1)],
                    start=(t == 0), stop=(t == NT - 1), skip_group_check=True))
        for b_ in range(4):
            q(V, lambda b_=b_: V.tensor_copy(
                F2[0:64, b_ * 272:(b_ + 1) * 272],
                psF[0:64, 1024 + b_ * 512:1024 + b_ * 512 + 272]))
        q(SY, lambda: SY.dma_start(out=abin, in_=F2[0:64, 0:1088]), dma=True)
        q(G, lambda: G.collective_compute(
            "AllReduce", ALU.add, replica_groups=[[0, 1], [2, 3], [4, 5], [6, 7]],
            ins=[abin.opt()], outs=[about.opt()]))
        q(SY, lambda: SY.dma_start(out=F2[0:64, 0:1088], in_=about), dma=True)
        q(G, lambda: G.dma_start(out=F2[64:128, 0:1088], in_=F2[0:64, 0:1088]),
          dma=True)
        for h in range(16):
            pb = 64 * (h % 2)
            sc0 = F2[pb:pb + 64, 68 * h + 64:68 * h + 65]
            r_ = MRC[pb:pb + 64, 0:1]
            msk = MRC[pb:pb + 64, 1:2]
            q(V, lambda sc0=sc0, msk=msk: V.tensor_scalar(msk, sc0, 0.0, None,
                                                          ALU.not_equal))
            q(V, lambda sc0=sc0, r_=r_: V.tensor_scalar(r_, sc0, 1e-38, None,
                                                        ALU.max))
            q(V, lambda r_=r_: V.reciprocal(r_, r_))
            q(V, lambda pb=pb, r_=r_, msk=msk: V.tensor_mul(
                MRC[pb:pb + 64, 2:3], r_, msk))
            q(V, lambda h=h, pb=pb: V.tensor_scalar(
                OUT2[pb:pb + 64, 64 * h:64 * (h + 1)],
                F2[pb:pb + 64, 68 * h:68 * h + 64],
                MRC[pb:pb + 64, 2:3], None, ALU.mult))
        AFM = ab[:, 32768:49152]

        def att_tile(t):
            return ATT[:, t * 1024:(t + 1) * 1024]

        tm2fm(att_tile, AFM, 1024)
        OFM = ab[:, 49152:65536]
        for h in range(16):
            for sp in range(4):
                q(T, lambda h=h, sp=sp: T.matmul(
                    psF[0:64, 0:512],
                    OUT2[64 * (h % 2):64 * (h % 2) + 64, 64 * h:64 * (h + 1)],
                    AFM[64 * (h % 2):64 * (h % 2) + 64,
                        (h // 2) * 2048 + sp * 512:(h // 2) * 2048 + (sp + 1) * 512],
                    start=True, stop=True))
                q(V, lambda h=h, sp=sp: V.tensor_copy(
                    OFM[64 * (h % 2):64 * (h % 2) + 64,
                        (h // 2) * 2048 + sp * 512:(h // 2) * 2048 + (sp + 1) * 512],
                    psF[0:64, 0:512]))
        WOT = ab[:, 0:8192]
        BO = ab[:, 8192:9216]
        cast_w_stream(WOT, wI["l0ab_woutT"], D, D)
        bias_row(BO[0:1, 0:1024], wI["l0ab_bout"], 1024)
        for t in range(NT):
            for din in range(8):
                for bk in range(2):
                    q(T, lambda t=t, din=din, bk=bk: T.matmul(
                        psF[:, bk * 512:(bk + 1) * 512],
                        OFM[:, din * TPC + t * 128:din * TPC + (t + 1) * 128],
                        WOT[:, din * 1024 + bk * 512:din * 1024 + (bk + 1) * 512],
                        start=(din == 0), stop=False))
            for bk in range(2):
                q(T, lambda bk=bk: T.matmul(
                    psF[:, bk * 512:(bk + 1) * 512], onesb[0:1, 0:128],
                    BO[0:1, bk * 512:(bk + 1) * 512], start=False, stop=True))
            residual_out(t, psF[:, 0:1024])

    def s5_l1aa():
        XNT = ab[:, 0:16384]
        FM1 = ab[:, 16384:32768]
        WQ3 = ab[:, 32768:57344]
        QTL = ab[:, 57344:58368]
        KTL = ab[:, 58368:59392]
        KPC = ab[:, 59392:63488]

        def xn_tile(t):
            return XNT[:, t * 1024:(t + 1) * 1024]

        prelude(xn_tile, src=xd)
        tm2fm(xn_tile, FM1, 1024)
        cast_w_stream(WQ3[:, 0:8192], wI["l1aa_wqT"], D, D)
        cast_w_stream(WQ3[:, 8192:16384], wI["l1aa_wkT"], D, D, scale=1.0 / 32.0)
        cast_w_stream(WQ3[:, 16384:24576], wI["l1aa_wvT"], D, D)

        QFM = ab[:, 0:16384]
        for t in range(NT):
            for pj in range(3):
                for din in range(8):
                    for bk in range(2):
                        q(T, lambda t=t, pj=pj, din=din, bk=bk: T.matmul(
                            psF[:, pj * 1024 + bk * 512:pj * 1024 + (bk + 1) * 512],
                            FM1[:, din * TPC + t * 128:din * TPC + (t + 1) * 128],
                            WQ3[:, pj * 8192 + din * 1024 + bk * 512:
                                pj * 8192 + din * 1024 + (bk + 1) * 512],
                            start=(din == 0), stop=(din == 7)))
            q(V, lambda: V.tensor_copy(QTL, psF[:, 0:1024]))
            q(V, lambda: V.tensor_copy(KTL, psF[:, 1024:2048]))
            for c in range(8):
                q(T, lambda c=c: T.transpose(psB[:, 0:128],
                                             QTL[:, c * 128:(c + 1) * 128],
                                             ident_bf[:]))
                q(V, lambda c=c, t=t: V.tensor_copy(
                    QFM[:, c * TPC + t * 128:c * TPC + (t + 1) * 128], psB[:, 0:128]))
                q(T, lambda c=c: T.transpose(psB[:, 128:256],
                                             KTL[:, c * 128:(c + 1) * 128],
                                             ident_bf[:]))
                q(V, lambda c=c, t=t: V.tensor_copy(
                    KPC[:, c * 512 + (t % 4) * 128:c * 512 + (t % 4 + 1) * 128],
                    psB[:, 128:256]))
            if t % 4 == 3:
                qb(SY, [(lambda c=c, t0=t - 3: SY.dma_start(
                    out=exin[1][c * 128 * TPC:(c + 1) * 128 * TPC]
                    .rearrange("(p t) -> p t", p=128)[:, t0 * 128:(t0 + 4) * 128],
                    in_=KPC[:, c * 512:(c + 1) * 512])) for c in range(8)],
                    dma=True)
            q(V, lambda: V.tensor_copy(QTL, psF[:, 2048:3072]))
            q(SY, lambda t=t: SY.dma_start(
                out=exin[1][D * TPC + t * 128 * D:D * TPC + (t + 1) * 128 * D]
                .rearrange("(p c) -> p c", p=128), in_=QTL), dma=True)

        exchange(1)

        KTS1 = ab[:, 16384:49152]
        ATB1 = ab[:, 49152:58368]
        VST = ab[:, 58368:62464]
        CEV = ab[:, 62464:64512]
        for c in range(8):
            qb(SY, [(lambda c=c, r=r: SY.dma_start(
                out=KTS1[:, c * 4096 + r * 2048:c * 4096 + (r + 1) * 2048],
                in_=exout[1][r * EXN + c * 128 * TPC:r * EXN + (c + 1) * 128 * TPC]
                .rearrange("(p t) -> p t", p=128))) for r in (0, 1)], dma=True)

        for p in range(8):
            nkt = 4 * p + 4
            qsl = WF[:, 0:256]
            q(SY, lambda p=p: SY.dma_start(out=qsl,
                                           in_=qidxb[:, 256 * p:256 * (p + 1)]),
              dma=True)
            for m in range(4):
                jg = 4 * p + m
                q(V, lambda m=m, jg=jg: V.tensor_scalar(
                    WF[:, 256 + m * 256:256 + (m + 1) * 256], qsl,
                    IOT[:, jg:jg + 1], -BIG, ALU.is_lt, ALU.mult))
            for ktj in range(nkt):
                r, sl = _slot[ktj]
                pss = psF[:, 2048 + (ktj % 2) * 256:2048 + (ktj % 2) * 256 + 256]
                for din in range(8):
                    q(T, lambda r=r, sl=sl, din=din, p=p, pss=pss: T.matmul(
                        pss,
                        KTS1[:, din * 4096 + r * 2048 + sl * 128:
                             din * 4096 + r * 2048 + (sl + 1) * 128],
                        QFM[:, din * TPC + 256 * p:din * TPC + 256 * (p + 1)],
                        start=(din == 0), stop=(din == 7)))
                if ktj >= 4 * p:
                    m = ktj - 4 * p
                    q(V, lambda pss=pss, m=m: V.tensor_tensor(
                        pss, pss, WF[:, 256 + m * 256:256 + (m + 1) * 256], ALU.add))
                q(Sc, lambda ktj=ktj, pss=pss: Sc.activation(
                    ATB1[:, ktj * 256:(ktj + 1) * 256], pss, AF.Exp))
            for dcol in range(8):
                for r in (0, 1):
                    q(SY, lambda r=r, dcol=dcol: SY.dma_start(
                        out=VST.rearrange("p (k c) -> p k c", c=128)
                        [:, 16 * r:16 * (r + 1), :],
                        in_=exout[1][r * EXN + D * TPC:r * EXN + 2 * D * TPC]
                        .rearrange("(t p c) -> p t c", p=128, c=D)
                        [:, :, dcol * 128:(dcol + 1) * 128]), dma=True)
                for ktj in range(nkt):
                    r, sl = _slot[ktj]
                    q(T, lambda r=r, sl=sl, ktj=ktj, dcol=dcol, nkt=nkt: T.matmul(
                        psF[:, dcol * 256:(dcol + 1) * 256],
                        VST[:, (16 * r + sl) * 128:(16 * r + sl + 1) * 128],
                        ATB1[:, ktj * 256:(ktj + 1) * 256],
                        start=(ktj == 0), stop=(ktj == nkt - 1),
                        skip_group_check=True))
            for ktj in range(nkt):
                r, sl = _slot[ktj]
                q(T, lambda r=r, sl=sl, ktj=ktj, nkt=nkt: T.matmul(
                    psF[0:1, 2560:2816], onesb[:, 0:1],
                    ATB1[:, ktj * 256:(ktj + 1) * 256],
                    start=(ktj == 0), stop=(ktj == nkt - 1), skip_group_check=True))
            q(V, lambda: V.tensor_copy(urow[0:1, 0:256], psF[0:1, 2560:2816]))
            for dcol in range(8):
                q(V, lambda dcol=dcol: V.tensor_copy(
                    CEV[:, dcol * 256:(dcol + 1) * 256],
                    psF[:, dcol * 256:(dcol + 1) * 256]))
            for th in range(2):
                t = 2 * p + th
                q(T, lambda th=th: T.transpose(
                    psF[:, 2816:2817], urow[0:1, th * 128:(th + 1) * 128],
                    ident32[0:1, 0:1]))
                dc_ = MRC[:, 6:7]
                q(V, lambda: V.tensor_scalar(dc_, psF[:, 2816:2817], 1e-38, None,
                                             ALU.max))
                q(V, lambda: V.reciprocal(dc_, dc_))
                q(SY, lambda t=t: SY.dma_start(out=XT,
                                               in_=xd[t * 128:(t + 1) * 128, :]),
                  dma=True)
                for dcol in range(8):
                    q(T, lambda dcol=dcol, th=th: T.transpose(
                        psB[:, 0:128],
                        CEV[0:128, dcol * 256 + th * 128:dcol * 256 + (th + 1) * 128],
                        ident_bf[:]))
                    q(V, lambda: V.tensor_copy(T2[:, 0:128], psB[:, 0:128]))
                    q(V, lambda dcol=dcol: V.scalar_tensor_tensor(
                        XT[:, dcol * 128:(dcol + 1) * 128], T2[:, 0:128], dc_,
                        XT[:, dcol * 128:(dcol + 1) * 128], ALU.mult, ALU.add))
                q(SY, lambda t=t: SY.dma_start(out=xd[t * 128:(t + 1) * 128, :],
                                               in_=XT), dma=True)

    def s7_l1ab():
        XNT = ab[:, 0:16384]
        FM1 = ab[:, 16384:32768]
        W3 = ab[:, 32768:57344]
        QTL = ab[:, 57344:58368]
        KPC = ab[:, 59392:63488]
        BR3 = ab[:, 63488:66560]

        def xn_tile(t):
            return XNT[:, t * 1024:(t + 1) * 1024]

        prelude(xn_tile, src=xd, park_h=True, km_cols=KMC)
        tm2fm(xn_tile, FM1, 1024)
        q(SY, lambda: SY.dma_start(out=urow[0:1, 1100:2124], in_=wI["l1ab_gamma"]),
          dma=True)
        q(SY, lambda: SY.dma_start(out=urow[0:1, 2130:3154], in_=wI["l1ab_beta"]),
          dma=True)
        for c0, base in ((1100, 0), (2130, 1024)):
            for hf in range(2):
                q(T, lambda c0=c0, hf=hf: T.matmul(
                    psF[:, 0:512], ones32[0:1, 0:128],
                    urow[0:1, c0 + hf * 512:c0 + (hf + 1) * 512],
                    start=True, stop=True))
                q(V, lambda base=base, hf=hf: V.tensor_copy(
                    F2[:, base + hf * 512:base + (hf + 1) * 512], psF[:, 0:512]))
        cast_w_stream(W3[:, 0:8192], wI["l1ab_wq"], D, D)
        cast_w_stream(W3[:, 8192:16384], wI["l1ab_wk"], D, D, scale=0.125)
        cast_w_stream(W3[:, 16384:24576], wI["l1ab_wv"], D, D)
        bias_row(BR3[0:1, 0:1024], wI["l1ab_bq"], 1024)
        bias_row(BR3[0:1, 1024:2048], wI["l1ab_bk"], 1024, scale_slice=(0, 1024))
        bias_row(BR3[0:1, 2048:3072], wI["l1ab_bv"], 1024)

        QFM = ab[:, 0:16384]
        for t in range(NT):
            for pj in range(3):
                for din in range(8):
                    for bk in range(2):
                        q(T, lambda t=t, pj=pj, din=din, bk=bk: T.matmul(
                            psF[:, pj * 1024 + bk * 512:pj * 1024 + (bk + 1) * 512],
                            FM1[:, din * TPC + t * 128:din * TPC + (t + 1) * 128],
                            W3[:, pj * 8192 + din * 1024 + bk * 512:
                               pj * 8192 + din * 1024 + (bk + 1) * 512],
                            start=(din == 0), stop=False))
                for bk in range(2):
                    q(T, lambda t=t, pj=pj, bk=bk: T.matmul(
                        psF[:, pj * 1024 + bk * 512:pj * 1024 + (bk + 1) * 512],
                        onesb[0:1, 0:128],
                        BR3[0:1, pj * 1024 + bk * 512:pj * 1024 + (bk + 1) * 512],
                        start=False, stop=True))
            q(Sc, lambda: Sc.activation(QTL, psF[:, 0:1024], AF.Relu))
            for c in range(8):
                q(T, lambda c=c: T.transpose(psB[:, 0:128],
                                             QTL[:, c * 128:(c + 1) * 128],
                                             ident_bf[:]))
                q(V, lambda c=c, t=t: V.tensor_copy(
                    QFM[:, c * TPC + t * 128:c * TPC + (t + 1) * 128], psB[:, 0:128]))
            q(Sc, lambda: Sc.activation(QTL, psF[:, 1024:2048], AF.Relu))
            for c in range(8):
                q(T, lambda c=c: T.transpose(psB[:, 128:256],
                                             QTL[:, c * 128:(c + 1) * 128],
                                             ident_bf[:]))
                q(V, lambda c=c, t=t: V.tensor_copy(
                    KPC[:, c * 512 + (t % 4) * 128:c * 512 + (t % 4 + 1) * 128],
                    psB[:, 128:256]))
            if t % 4 == 3:
                qb(SY, [(lambda c=c, t0=t - 3: SY.dma_start(
                    out=exin[2][c * 128 * TPC:(c + 1) * 128 * TPC]
                    .rearrange("(p t) -> p t", p=128)[:, t0 * 128:(t0 + 4) * 128],
                    in_=KPC[:, c * 512:(c + 1) * 512])) for c in range(8)],
                    dma=True)
            q(Sc, lambda: Sc.activation(QTL, psF[:, 2048:3072], AF.Relu))
            q(SY, lambda t=t: SY.dma_start(
                out=exin[2][D * TPC + t * 128 * D:D * TPC + (t + 1) * 128 * D]
                .rearrange("(p c) -> p c", p=128), in_=QTL), dma=True)
        q(V, lambda: V.tensor_copy(ab[:, 57344:57360], KMC))
        q(SY, lambda: SY.dma_start(
            out=exin[2][2 * D * TPC:2 * D * TPC + 2048]
            .rearrange("(p c) -> p c", p=128), in_=ab[:, 57344:57360]), dma=True)

        exchange(2)

        KTS = ab[:, 16384:20480]
        VS = ab[:, 20480:24640]
        ATB = ab[:, 24640:26176]
        CEV = ab[:, 26176:26688]
        KMFB = ab[:, 26688:26720]
        CTX = ab[:, 32768:49152]
        KMF = util[:, 216:248]
        for r in (0, 1):
            q(SY, lambda r=r: SY.dma_start(
                out=KMFB[:, r * 16:(r + 1) * 16],
                in_=exout[2][r * EXN + 2 * D * TPC:r * EXN + 2 * D * TPC + 2048]
                .rearrange("(p c) -> p c", p=128)), dma=True)
        q(V, lambda: V.tensor_copy(KMF, KMFB))
        q(V, lambda: V.tensor_scalar(KMB, KMF, BIG, BIG, ALU.mult, ALU.subtract))

        for hg in range(8):
            unpack_kts(2, hg, KTS)
            unpack_vs(2, hg * 2, 2, VS, 130)
            for hh in range(2):
                pb = 64 * hh
                for p in range(8):
                    nkt = 4 * p + 4
                    qsl = WF[:, 0:256]
                    q(SY, lambda p=p: SY.dma_start(
                        out=qsl, in_=qidxb[:, 256 * p:256 * (p + 1)]), dma=True)
                    for m in range(4):
                        jg = 4 * p + m
                        q(V, lambda m=m, jg=jg: V.tensor_scalar(
                            WF[:, 256 + m * 256:256 + (m + 1) * 256], qsl,
                            IOT[:, jg:jg + 1], -BIG, ALU.is_lt, ALU.mult))
                    for ktj in range(nkt):
                        r, sl = _slot[ktj]
                        kcol = r * 2048 + sl * 128
                        vcol = (16 * r + sl) * 130 + 65 * hh
                        kmcol = 16 * r + sl
                        pss = psF[:, 2048 + (ktj % 2) * 256:
                                  2048 + (ktj % 2) * 256 + 256]
                        atb = ATB[:, (ktj % 3) * 512:(ktj % 3) * 512 + 256]
                        q(T, lambda pss=pss, kcol=kcol, pb=pb, hg=hg, p=p: T.matmul(
                            pss, KTS[pb:pb + 64, kcol:kcol + 128],
                            QFM[pb:pb + 64,
                                hg * 2048 + 256 * p:hg * 2048 + 256 * (p + 1)],
                            start=True, stop=True))
                        if ktj >= 4 * p:
                            m = ktj - 4 * p
                            q(V, lambda pss=pss, m=m: V.tensor_tensor(
                                pss, pss, WF[:, 256 + m * 256:256 + (m + 1) * 256],
                                ALU.add))
                        q(Sc, lambda atb=atb, pss=pss, kmcol=kmcol: Sc.activation(
                            atb, pss, AF.Exp, bias=KMB[:, kmcol:kmcol + 1]))
                        q(T, lambda vcol=vcol, atb=atb, ktj=ktj, nkt=nkt: T.matmul(
                            psF[0:65, 0:256], VS[:, vcol:vcol + 65], atb,
                            start=(ktj == 0), stop=(ktj == nkt - 1)))
                    q(V, lambda: V.tensor_copy(CEV[0:65, 0:256], psF[0:65, 0:256]))
                    for th in range(2):
                        t = 2 * p + th
                        q(T, lambda th=th: T.transpose(
                            psB[:, 0:65], CEV[0:65, th * 128:(th + 1) * 128],
                            ident_bf[0:65, 0:65]))
                        q(V, lambda: V.tensor_copy(T1[:, 0:65], psB[:, 0:65]))
                        dc_ = MRC[:, 6:7]
                        q(V, lambda: V.tensor_scalar(dc_, T1[:, 64:65], 1e-38,
                                                     None, ALU.max))
                        q(V, lambda: V.reciprocal(dc_, dc_))
                        q(V, lambda t=t: V.tensor_mul(dc_, dc_, KMC[:, t:t + 1]))
                        hq_ = 2 * hg + hh
                        q(V, lambda t=t, hq_=hq_: V.tensor_scalar(
                            CTX[:, t * 1024 + hq_ * 64:t * 1024 + hq_ * 64 + 64],
                            T1[:, 0:64], dc_, None, ALU.mult))
        for t in range(NT):
            q(SY, lambda t=t: SY.dma_start(out=HT, in_=hd[t * 128:(t + 1) * 128, :]),
              dma=True)
            q(V, lambda t=t: V.tensor_copy(T1, CTX[:, t * 1024:(t + 1) * 1024]))
            q(V, lambda: V.tensor_add(T2, T1, HT))
            ln_tile(T2, T1, eps=1e-8)
            q(V, lambda: V.tensor_mul(T1, T1, F2[:, 0:1024]))
            q(V, lambda: V.tensor_add(T1, T1, F2[:, 1024:2048]))
            q(SY, lambda t=t: SY.dma_start(out=XT, in_=xd[t * 128:(t + 1) * 128, :]),
              dma=True)
            q(V, lambda: V.tensor_add(T1, T1, XT))
            q(SY, lambda t=t: SY.dma_start(out=xd[t * 128:(t + 1) * 128, :],
                                           in_=T1), dma=True)

    def s_ffn1(first):
        XQT = ab[:, 0:16384]
        XQF = ab[:, 16384:32768]
        WBFC = ab[:, 32768:35584]
        WGS = ab[:, 35584:39680]
        WUS = ab[:, 39680:43776]

        def xq_tile(t):
            return XQT[:, t * 1024:(t + 1) * 1024]

        prelude(xq_tile, src=xd,
                quant=lambda t, src, dst: fp4_qd(src, dst, 1024,
                                                 F2[:, 0:1024], WF[:, 0:1024],
                                                 66000))
        tm2fm(xq_tile, XQF, 1024)

        if first:
            for nm, ap_, rows, cols in [("g", wI["l1f_wg"], 2816, 1024),
                                        ("u", wI["l1f_wu"], 2816, 1024),
                                        ("d", wI["l1f_wd"], 1024, 2816)]:
                for c in range(rows // 128):
                    q(SY, lambda c=c, ap_=ap_, cols=cols: SY.dma_start(
                        out=WF[:, 0:cols], in_=ap_[c * 128:(c + 1) * 128, :]),
                      dma=True)
                    fp4_qd(WF[:, 0:cols], WBFC[:, 0:cols], cols,
                           F2[:, 0:cols], af[:, 0:cols], 43776)
                    q(SY, lambda c=c, nm=nm, cols=cols: SY.dma_start(
                        out=wq1[nm][c * 128:(c + 1) * 128, :],
                        in_=WBFC[:, 0:cols]), dma=True)

        ICW = [512, 512, 512, 512, 512, 256]
        for ic in range(6):
            icw = ICW[ic]
            ic0 = 512 * ic
            qb(SY, [(lambda din=din, ic0=ic0, icw=icw: SY.dma_start(
                out=WGS[:, din * 512:din * 512 + icw],
                in_=wq1["g"][ic0:ic0 + icw, din * 128:(din + 1) * 128],
                transpose=True)) for din in range(8)], dma=True)
            qb(SY, [(lambda din=din, ic0=ic0, icw=icw: SY.dma_start(
                out=WUS[:, din * 512:din * 512 + icw],
                in_=wq1["u"][ic0:ic0 + icw, din * 128:(din + 1) * 128],
                transpose=True)) for din in range(8)], dma=True)
            for t in range(NT):
                for din in range(8):
                    q(T, lambda t=t, din=din, icw=icw: T.matmul(
                        psF[:, 0:icw],
                        XQF[:, din * TPC + t * 128:din * TPC + (t + 1) * 128],
                        WGS[:, din * 512:din * 512 + icw],
                        start=(din == 0), stop=(din == 7)))
                    q(T, lambda t=t, din=din, icw=icw: T.matmul(
                        psF[:, 512:512 + icw],
                        XQF[:, din * TPC + t * 128:din * TPC + (t + 1) * 128],
                        WUS[:, din * 512:din * 512 + icw],
                        start=(din == 0), stop=(din == 7)))
                q(V, lambda icw=icw: V.tensor_copy(T1[:, 0:icw], psF[:, 0:icw]))
                q(Sc, lambda icw=icw: Sc.activation(T2[:, 0:icw], T1[:, 0:icw],
                                                    AF.Silu))
                q(V, lambda icw=icw: V.scalar_tensor_tensor(
                    T1[:, 0:icw], psF[:, 512:512 + icw], 1.0, T2[:, 0:icw],
                    ALU.mult, ALU.mult))
                q(SY, lambda t=t, ic0=ic0, icw=icw: SY.dma_start(
                    out=hrows[t * 128:(t + 1) * 128, ic0:ic0 + icw],
                    in_=T1[:, 0:icw]), dma=True)

        WDT = ab[:, 0:22528]
        HQ = ab[:, 22528:25344]
        HQF = ab[:, 36608:39424]
        qb(SY, [(lambda din=din, bk=bk: SY.dma_start(
            out=WDT[:, din * 1024 + bk * 512:din * 1024 + (bk + 1) * 512],
            in_=wq1["d"][bk * 512:(bk + 1) * 512, din * 128:(din + 1) * 128],
            transpose=True)) for din in range(22) for bk in range(2)], dma=True)
        for t in range(NT):
            q(SY, lambda t=t: SY.dma_start(
                out=WF[:, 0:2816], in_=hrows[t * 128:(t + 1) * 128, 0:2816]),
              dma=True)
            fp4_qd(WF[:, 0:2816], HQ[:, 0:2816], 2816, F2[:, 0:2816],
                   af[:, 0:2816], 25344)
            for c0 in range(0, 22, 4):
                kk = min(4, 22 - c0)
                for k in range(kk):
                    q(T, lambda c0=c0, k=k: T.transpose(
                        psB[:, k * 128:(k + 1) * 128],
                        HQ[:, (c0 + k) * 128:(c0 + k + 1) * 128], ident_bf[:]))
                q(V, lambda c0=c0, kk=kk: V.tensor_copy(
                    HQF[:, c0 * 128:(c0 + kk) * 128], psB[:, 0:kk * 128]))
            for din in range(22):
                for bk in range(2):
                    q(T, lambda t=t, din=din, bk=bk: T.matmul(
                        psF[:, bk * 512:(bk + 1) * 512],
                        HQF[:, din * 128:(din + 1) * 128],
                        WDT[:, din * 1024 + bk * 512:din * 1024 + (bk + 1) * 512],
                        start=(din == 0), stop=(din == 21)))
            residual_out(t, psF[:, 0:1024])

    subs = [s1_l0aa,
            lambda: s_ffn0(True),
            s3_l0ab,
            lambda: s_ffn0(False),
            s5_l1aa,
            lambda: s_ffn1(True),
            s7_l1ab,
            lambda: s_ffn1(False)]
    for i in range(n_sub):
        subs[i]()

    q(SY, lambda: SY.dma_start(out=y, in_=xd), dma=True)
    PROG.append((id(SY), None, qc[0], 0))

    for eng, nm in ((T, "tensor"), (V, "vector"), (Sc, "scalar"),
                    (G, "gpsimd"), (SY, "sync")):
        items = [it for it in PROG if it[0] == id(eng)]

        def body(e, items=items):
            for _, fn, w, n in items:
                e.wait_ge(QS, w)
                if fn is None:
                    continue
                ins = fn()
                if n:
                    ins.then_inc(QS, n)
        getattr(blk, nm)(body)

    for cm in reversed(ctxs):
        cm.__exit__(None, None, None)
    return nc


_invf = None


def _mk_core_inputs(inputs, core):
    b, half = core // 2, core % 2
    blks = blocks_for(half)
    gidx = np.concatenate([np.arange(j * 128, (j + 1) * 128) for j in blks])
    x = np.asarray(inputs["L0_attn_a_X"], np.float32)[b][gidx]

    inv_freq = (1.0 / 10000.0 ** (np.arange(0, 32, 2, dtype=np.float32) / 32.0)
                ).astype(np.float32)
    f = (gidx.astype(np.float32)[:, None] * inv_freq[None, :]).astype(np.float32)
    cos = np.cos(f).astype(np.float32)
    sin = np.sin(f).astype(np.float32)
    cosr = np.tile(cos, (1, 16))
    sinr = np.tile(sin, (1, 16))
    cosrep = np.concatenate([cosr, sinr], 1).astype(np.float32)

    qidxb = np.tile(gidx.astype(np.float32)[None, :], (128, 1))

    t = lambda a: np.ascontiguousarray(np.asarray(a, np.float32).T)
    c = lambda a: np.ascontiguousarray(np.asarray(a, np.float32))
    r = lambda a: np.asarray(a, np.float32).reshape(1, -1)
    m = {
        "x0": c(x), "ident": np.eye(128, dtype=np.float32),
        "cosrep": cosrep, "qidxb": qidxb.astype(np.float32),
        "l0aa_win": c(inputs["L0_attn_a_W_in"]),
        "l0aa_bin": r(inputs["L0_attn_a_b_in"]),
        "l0aa_wout": c(inputs["L0_attn_a_W_out"]),
        "l0aa_bout": r(inputs["L0_attn_a_b_out"]),
        "l0aa_gamma": r(inputs["L0_attn_a_gamma"]),
        "l0aa_beta": r(inputs["L0_attn_a_beta"]),
        "l0ab_wembT": t(inputs["L0_attn_b_W_emb"]),
        "l0ab_wanchT": t(inputs["L0_attn_b_W_anchor"]),
        "l0ab_woutT": t(inputs["L0_attn_b_W_out"]),
        "l0ab_bout": r(inputs["L0_attn_b_b_out"]),
        "l0f_wgT": t(inputs["L0_ffn_w_gate"]),
        "l0f_wuT": t(inputs["L0_ffn_w_up"]),
        "l0f_wdT": t(inputs["L0_ffn_w_down"]),
        "l1aa_wqT": t(inputs["L1_attn_a_Wq"]),
        "l1aa_wkT": t(inputs["L1_attn_a_Wk"]),
        "l1aa_wvT": t(inputs["L1_attn_a_Wv"]),
        "l1ab_wq": c(inputs["L1_attn_b_Wq"]),
        "l1ab_wk": c(inputs["L1_attn_b_Wk"]),
        "l1ab_wv": c(inputs["L1_attn_b_Wv"]),
        "l1ab_bq": r(inputs["L1_attn_b_bq"]),
        "l1ab_bk": r(inputs["L1_attn_b_bk"]),
        "l1ab_bv": r(inputs["L1_attn_b_bv"]),
        "l1ab_gamma": r(inputs["L1_attn_b_gamma"]),
        "l1ab_beta": r(inputs["L1_attn_b_beta"]),
        "l1f_wg": c(inputs["L1_ffn_w_gate"]),
        "l1f_wu": c(inputs["L1_ffn_w_up"]),
        "l1f_wd": c(inputs["L1_ffn_w_down"]),
    }
    return m


def kernel(**inputs):
    nc = build_program()
    in_maps = [_mk_core_inputs(inputs, c) for c in range(8)]
    res = run_bass_kernel_spmd(nc, in_maps, core_ids=list(range(8)))
    out = np.zeros((B, S, D), np.float32)
    for core in range(8):
        b, half = core // 2, core % 2
        yv = res.results[core]["y"]
        for s_, j in enumerate(blocks_for(half)):
            out[b, j * 128:(j + 1) * 128] = yv[s_ * 128:(s_ + 1) * 128]
    return out
